# version 06 (final)
# speedup vs baseline: 1.2520x; 1.0874x over previous
import numpy as np
import ml_dtypes
from contextlib import ExitStack

import concourse.bass as bass
import concourse.mybir as mybir
from concourse.bass_utils import run_bass_kernel_spmd

F32 = mybir.dt.float32
BF16 = mybir.dt.bfloat16
ALU = mybir.AluOpType
ACT = mybir.ActivationFunctionType
AXL = mybir.AxisListType
BF = ml_dtypes.bfloat16

D = 1024
H = 16
DH = 64
NCORES = 8
DHID = 4096
NSTAGE = 12


class S:

    def __init__(self, es, nc, name):
        self.h = es.enter_context(nc.semaphore(name))
        self.v = 0



def build_L1(T):
    nc = bass.Bass("TRN2", target_bir_lowering=False, detect_race_conditions=False)
    x_d = nc.dram_tensor("x", [T, D], F32, kind="ExternalInput").ap()
    idb_d = nc.dram_tensor("identb", [128, 128], BF16, kind="ExternalInput").ap()
    xnT_d = nc.dram_tensor("xnT", [128, 8 * T], BF16, kind="ExternalOutput").ap()
    dbg_d = nc.dram_tensor("dbg", [128, 8], F32, kind="ExternalOutput").ap()
    dbg2_d = nc.dram_tensor("dbg2", [128, D], BF16, kind="ExternalOutput").ap()

    nt = T // 128
    es = ExitStack()
    xt = es.enter_context(nc.sbuf_tensor("xt", [128, D], F32))
    xn = es.enter_context(nc.sbuf_tensor("xn", [128, D], BF16))
    st_ = es.enter_context(nc.sbuf_tensor("st", [128, 8], F32))
    stq = es.enter_context(nc.sbuf_tensor("stq", [128, 1], F32))
    stq6 = es.enter_context(nc.sbuf_tensor("stq6", [128, 24], F32))
    epsc = es.enter_context(nc.sbuf_tensor("epsc", [128, 1], F32))
    str2 = es.enter_context(nc.sbuf_tensor("str2", [128, 1], F32))
    strash = es.enter_context(nc.sbuf_tensor("strash", [128, D], F32))
    idb = es.enter_context(nc.sbuf_tensor("idb", [128, 128], BF16))
    xnT = es.enter_context(nc.sbuf_tensor("xnT_sb", [128, 8 * T], BF16))
    ptrA = es.enter_context(nc.psum_tensor("ptrA", [128, 1024], BF16))
    ptrB = es.enter_context(nc.psum_tensor("ptrB", [128, 1024], BF16))
    ld = S(es, nc, "ld")
    ssq = S(es, nc, "ssq")
    vvar = S(es, nc, "vvar")
    ssr = S(es, nc, "ssr")
    vxn = S(es, nc, "vxn")
    ptrs = S(es, nc, "ptrs")
    sev = S(es, nc, "sev")
    dst = S(es, nc, "dst")
    wocast = S(es, nc, "wocast")
    vfin = S(es, nc, "vfin")
    block = es.enter_context(nc.Block())

    @block.sync
    def _(sync):
        sync.dma_start(out=idb[:], in_=idb_d).then_inc(ld.h, 16)
        for i in range(nt):
            if i > 0:
                sync.wait_ge(vxn.h, i)
            sync.dma_start(out=xt[:], in_=x_d[i * 128:(i + 1) * 128, :]).then_inc(ld.h, 16)
        ld.v = 16 * (nt + 1)

    @block.scalar
    def _(scalar):
        for i in range(nt):
            scalar.wait_ge(vvar.h, i + 1)
            scalar.activation(str2[:, 0:1], st_[:, 4:5], ACT.Ln)
            scalar.activation(str2[:, 0:1], str2[:, 0:1], ACT.Exp, scale=-0.5)
            scalar.activation(strash[:, 0:1], str2[:, 0:1], ACT.Copy).then_inc(ssr.h, 1)

    @block.vector
    def _(vector):
        for i in range(nt):
            vector.wait_ge(ld.h, 16 * (i + 2))
            vector.reduce_sum(st_[:, 0:1], xt[:], axis=AXL.X)
            vector.scalar_tensor_tensor(strash[:], xt[:], 1.0, xt[:],
                                        ALU.mult, ALU.mult, accum_out=stq[:, 0:1])
            vector.tensor_scalar(st_[:, 1:2], st_[:, 0:1], -1.0 / D, None, ALU.mult)
            vector.tensor_scalar(st_[:, 2:3], stq[:, 0:1], 1.0 / D, 1e-5,
                                 ALU.mult, ALU.add)
            vector.tensor_tensor(st_[:, 3:4], st_[:, 1:2], st_[:, 1:2], ALU.mult)
            vector.tensor_copy(st_[:, 7:8], st_[:, 0:1])
            vector.tensor_tensor(st_[:, 4:5], st_[:, 2:3], st_[:, 3:4],
                                 ALU.subtract).then_inc(vvar.h, 1)
            vector.wait_ge(ssr.h, i + 1)
            if i > 0:
                vector.wait_ge(ptrs.h, 8 * i)
            vector.tensor_scalar(xn[:], xt[:], st_[:, 1:2], str2[:, 0:1],
                                 ALU.add, ALU.mult)
            vector.tensor_copy(st_[:, 5:6], st_[:, 4:5]).then_inc(vxn.h, 1)
            for k in range(8):
                vector.wait_ge(ptrs.h, 8 * i + k + 1)
                srcp = (ptrA if k % 2 == 0 else ptrB)
                vector.tensor_copy(xnT[:, k * T + i * 128: k * T + (i + 1) * 128],
                                   srcp[:, 0:128]).then_inc(sev.h, 1)
            vector.tensor_copy(st_[:, 6:7], st_[:, 4:5]).then_inc(vfin.h, 1)

    @block.tensor
    def _(tensor):
        tensor.wait_ge(ld.h, 16)
        for i in range(nt):
            tensor.wait_ge(vxn.h, i + 1)
            for k in range(8):
                n = 8 * i + k
                if n >= 1:
                    tensor.wait_ge(sev.h, n)
                dstp = (ptrA if k % 2 == 0 else ptrB)
                tensor.transpose(dstp[:, 0:128],
                                 xn[:, k * 128:(k + 1) * 128], idb[:])
                tensor.drain().then_inc(ptrs.h, 1)

    @block.gpsimd
    def _(gpsimd):
        gpsimd.wait_ge(sev.h, 8 * nt)
        gpsimd.wait_ge(vfin.h, nt)
        gpsimd.dma_start(out=xnT_d, in_=xnT[:]).then_inc(dst.h, 16)
        gpsimd.dma_start(out=dbg_d, in_=st_[:, 0:8]).then_inc(dst.h, 16)
        gpsimd.dma_start(out=dbg2_d, in_=xn[:]).then_inc(dst.h, 16)
        gpsimd.wait_ge(dst.h, 48)

    es.close()
    return nc


def host_L1_inputs(x_slice):
    ident = np.eye(128, dtype=np.float32).astype(BF)
    return {"x": np.ascontiguousarray(x_slice), "identb": ident}


_CACHE = {}


def _cached(key, builder):
    if key not in _CACHE:
        _CACHE[key] = builder()
    return _CACHE[key]


def kernel(**inputs):
    from concourse.bass_utils import run_bass_kernel_spmd
    LTOT = 16384
    T = LTOT // 8
    x3 = np.asarray(inputs["L0_attn_x"])
    x = np.ascontiguousarray(x3.reshape(LTOT, D))
    Wq = np.asarray(inputs["L0_attn_Wq"]); Wk = np.asarray(inputs["L0_attn_Wk"])
    Wv = np.asarray(inputs["L0_attn_Wv"]); Wg = np.asarray(inputs["L0_attn_Wg"])
    Wb = np.asarray(inputs["L0_attn_Wb"]); Wa = np.asarray(inputs["L0_attn_Wa"])
    dt_bias = np.asarray(inputs["L0_attn_dt_bias"])
    A_log = np.asarray(inputs["L0_attn_A_log"])
    norm_w = np.asarray(inputs["L0_attn_norm_w"])
    Wo = np.asarray(inputs["L0_attn_Wo"]); bo = np.asarray(inputs["L0_attn_bo"])
    Wexp = np.asarray(inputs["L0_ffn_W_exp"]); b_exp = np.asarray(inputs["L0_ffn_b_exp"])
    upw = np.asarray(inputs["L0_ffn_up_weights"])
    up_bias = np.asarray(inputs["L0_ffn_up_bias"])
    dnw = np.asarray(inputs["L0_ffn_down_weights"])
    Wcon = np.asarray(inputs["L0_ffn_W_con"]); b_con = np.asarray(inputs["L0_ffn_b_con"])
    down_bias = np.asarray(inputs["L0_ffn_down_bias"])
    cores = list(range(8))

    nc1 = _cached("L1", lambda: build_L1(T))
    r1 = run_bass_kernel_spmd(nc1, [host_L1_inputs(x[c * T:(c + 1) * T]) for c in cores],
                              core_ids=cores)
    xnT_c = [np.asarray(r1.results[c]["xnT"]) for c in cores]
    xnTf = np.zeros((128, 8 * LTOT), BF)
    for kd in range(8):
        for c in cores:
            xnTf[:, kd * LTOT + c * T: kd * LTOT + (c + 1) * T] = \
                xnT_c[c][:, kd * T:(kd + 1) * T]

    ut = build_stage_tiles(upw); dt = build_stage_tiles(dnw)
    um, up_packs = meta_up(ut); dm, dn_packs = meta_dn(dt)
    nc3a = _cached("L3a", lambda: build_L3a(um, dm, 129, 128))
    WexpT = np.ascontiguousarray(Wexp.T)
    r3a = run_bass_kernel_spmd(
        nc3a,
        [host_L3a_inputs(c, WexpT, Wcon, b_exp, up_packs, dn_packs, Wo) for c in cores],
        core_ids=cores)
    wexpP = np.concatenate([np.asarray(r3a.results[c]["wexpP"]) for c in cores], axis=0)
    wconP = np.concatenate([np.asarray(r3a.results[c]["wconP"]) for c in cores], axis=1)
    bcol = np.asarray(r3a.results[0]["bcol"])
    woT = np.asarray(r3a.results[0]["woT"])

    l2maps = [dict(host_L2_inputs(c, Wq, Wk, Wv, Wb, Wg, Wa, dt_bias, A_log, norm_w))
              for c in cores]
    ctxfT = np.zeros((D, LTOT), BF)
    sst = [np.zeros((128, 128), np.float32) for _ in cores]
    off = 0
    for Lc in (6144, 6144, 4096):
        ncl = _cached(f"L2_{Lc}", lambda Lc=Lc: build_L2(Lc))
        maps = []
        for c in cores:
            xs = np.zeros((128, 8 * Lc), BF)
            for kd in range(8):
                xs[:, kd * Lc:(kd + 1) * Lc] = \
                    xnTf[:, kd * LTOT + off: kd * LTOT + off + Lc]
            m = dict(l2maps[c]); m["xnTf"] = xs; m["sst0"] = sst[c]
            maps.append(m)
        rl = run_bass_kernel_spmd(ncl, maps, core_ids=cores)
        for c in cores:
            ctxfT[c * 128:(c + 1) * 128, off:off + Lc] = np.asarray(rl.results[c]["ctxT"])
            sst[c] = np.asarray(rl.results[c]["sstf"])
        off += Lc

    nc3b = _cached("L3b", lambda: build_L3b(T))
    upb = np.ascontiguousarray(up_bias.reshape(32, 128).T).astype(np.float32)
    bvecs = np.ascontiguousarray(
        np.concatenate([bo, b_con, down_bias]).reshape(1, 3 * D)).astype(np.float32)
    ones1 = np.ones((1, 128), np.float32)
    maps = []
    for c in cores:
        maps.append({
            "x": np.ascontiguousarray(x[c * T:(c + 1) * T]),
            "xnT": xnT_c[c],
            "ctxT": np.ascontiguousarray(ctxfT[:, c * T:(c + 1) * T]),
            "wexpP": wexpP, "wconP": wconP, "woT": woT,
            "bcol": bcol, "upb": upb, "bvecs": bvecs, "ones1": ones1,
        })
    r3b = run_bass_kernel_spmd(nc3b, maps, core_ids=cores)
    out = np.concatenate([np.asarray(r3b.results[c]["out"]) for c in cores], axis=0)
    return out.reshape(1, LTOT, D)



def build_stage_tiles(weights, dim=DHID):
    nst = weights.shape[0]
    out = []
    for s in range(nst):
        sz = 2 ** s
        nb = dim // (2 * sz)
        w = weights[s].reshape(nb, sz, 2, 2)
        dvec = np.zeros(dim, np.float32)
        uvec = np.zeros(dim, np.float32)
        lvec = np.zeros(dim, np.float32)
        for blk in range(nb):
            ia = blk * 2 * sz + np.arange(sz)
            ib = ia + sz
            dvec[ia] = w[blk, :, 0, 0]
            dvec[ib] = w[blk, :, 1, 1]
            uvec[ia] = w[blk, :, 0, 1]
            lvec[ib] = w[blk, :, 1, 0]
        tiles = {}
        ntile = dim // 128
        for RB in range(ntile):
            rows = np.arange(RB * 128, (RB + 1) * 128)
            blockrow = np.zeros((128, dim), np.float32)
            blockrow[np.arange(128), rows] = dvec[rows]
            m = (rows + sz < dim) & (uvec[rows] != 0)
            blockrow[np.arange(128)[m], rows[m] + sz] = uvec[rows[m]]
            m2 = (rows - sz >= 0) & (lvec[rows] != 0)
            blockrow[np.arange(128)[m2], rows[m2] - sz] = lvec[rows[m2]]
            for CB in range(ntile):
                t = blockrow[:, CB * 128:(CB + 1) * 128]
                if np.any(t):
                    tiles[(RB, CB)] = np.ascontiguousarray(t)
        out.append(tiles)
    return out


def meta_up(stage_tiles):
    metas, packs = [], []
    for tiles in stage_tiles:
        lst = sorted(tiles.keys())
        meta = [(RB, CB) for (RB, CB) in lst]
        metas.append(meta)
        packs.append([np.ascontiguousarray(tiles[k].T) for k in lst])
    return metas, packs


def meta_dn(stage_tiles):
    metas, packs = [], []
    for tiles in reversed(stage_tiles):
        lst = sorted(((CB, RB) for (RB, CB) in tiles.keys()))
        metas.append(list(lst))
        packs.append([np.ascontiguousarray(tiles[(RB, CB)]) for (CB, RB) in lst])
    return metas, packs


def np_butterfly(y, weights):
    dim = y.shape[-1]
    lead = y.shape[:-1]
    for stage in range(weights.shape[0]):
        s = 2 ** stage
        nb = dim // (2 * s)
        yr = y.reshape(*lead, nb, 2, s)
        a = yr[..., 0, :]
        b = yr[..., 1, :]
        w = weights[stage].reshape(nb, s, 2, 2)
        na = w[..., 0, 0] * a + w[..., 0, 1] * b
        nb2 = w[..., 1, 0] * a + w[..., 1, 1] * b
        y = np.stack([na, nb2], axis=-2).reshape(*lead, dim)
    return y



def build_L3a(up_meta, dn_meta, W1, W2):
    nc = bass.Bass("TRN2", target_bir_lowering=False, detect_race_conditions=False)
    nup = sum(len(v) for v in up_meta)
    ndn = sum(len(v) for v in dn_meta)
    vup0_d = nc.dram_tensor("vup0", [DHID, W1], F32, kind="ExternalInput").ap()
    vdn0_d = nc.dram_tensor("vdn0", [DHID, W2], F32, kind="ExternalInput").ap()
    upt_d = nc.dram_tensor("uptiles", [nup * 128, 128], F32, kind="ExternalInput").ap()
    dnt_d = nc.dram_tensor("dntiles", [ndn * 128, 128], F32, kind="ExternalInput").ap()
    idb_d = nc.dram_tensor("identb", [128, 128], BF16, kind="ExternalInput").ap()
    wexpP_d = nc.dram_tensor("wexpP", [128, DHID], BF16, kind="ExternalOutput").ap()
    bcol_d = nc.dram_tensor("bcol", [128, 32], F32, kind="ExternalOutput").ap()
    wconP_d = nc.dram_tensor("wconP", [DHID, W2], BF16, kind="ExternalOutput").ap()
    woTf_d = nc.dram_tensor("woTf", [D, D], F32, kind="ExternalInput").ap()
    woT_d = nc.dram_tensor("woT", [D, D], BF16, kind="ExternalOutput").ap()

    es = ExitStack()
    vA = es.enter_context(nc.sbuf_tensor("vA", [128, 32 * W1], F32))
    vB = es.enter_context(nc.sbuf_tensor("vB", [128, 32 * W1], F32))
    tbuf = es.enter_context(nc.sbuf_tensor("tbuf", [128, 8 * 128], F32))
    outP = es.enter_context(nc.sbuf_tensor("outP", [128, DHID], BF16))
    outC = es.enter_context(nc.sbuf_tensor("outC", [128, 32 * W2], BF16))
    idb = es.enter_context(nc.sbuf_tensor("idb", [128, 128], BF16))
    vfb = es.enter_context(nc.sbuf_tensor("vfb", [128, 128], BF16))
    bcol_sb = es.enter_context(nc.sbuf_tensor("bcol_sb", [128, 32], F32))
    wocf = es.enter_context(nc.sbuf_tensor("wocf", [128, D], F32))
    wocb = es.enter_context(nc.sbuf_tensor("wocb", [128, D], BF16))
    pacc = es.enter_context(nc.psum_tensor("pacc", [128, 512], F32))
    pacc2 = es.enter_context(nc.psum_tensor("pacc2", [128, 512], F32))
    ptr2 = es.enter_context(nc.psum_tensor("ptr2", [128, 1024], BF16))
    ld = S(es, nc, "ld")
    tl = S(es, nc, "tl")
    mm = S(es, nc, "mm")
    ev = S(es, nc, "ev")
    fin = S(es, nc, "fin")
    upd = S(es, nc, "upd")
    trp = S(es, nc, "trp")
    woc = S(es, nc, "woc")
    dst = S(es, nc, "dst")
    block = es.enter_context(nc.Block())

    up_seq = [(s, X, Y) for s, lst in enumerate(up_meta) for (X, Y) in lst]
    dn_seq = [(s, X, Y) for s, lst in enumerate(dn_meta) for (X, Y) in lst]
    nseq = len(up_seq) + len(dn_seq)

    @block.sync
    def _(sync):
        sync.dma_start(out=idb[:], in_=idb_d).then_inc(ld.h, 16)
        for ft in range(32):
            sync.dma_start(out=vA[:, ft * W1:(ft + 1) * W1],
                           in_=vup0_d[ft * 128:(ft + 1) * 128, :]).then_inc(ld.h, 16)
        for sq in range(len(up_seq)):
            if sq >= 8:
                sync.wait_ge(mm.h, sq - 7)
            sync.dma_start(out=tbuf[:, (sq % 8) * 128:(sq % 8) * 128 + 128],
                           in_=upt_d[sq * 128:sq * 128 + 128, :]).then_inc(tl.h, 16)
        for kb in range(8):
            if kb > 0:
                sync.wait_ge(woc.h, 33 * kb - 16)
            sync.dma_start(out=wocf[:], in_=woTf_d[kb * 128:(kb + 1) * 128, :]).then_inc(woc.h, 16)
            sync.wait_ge(woc.h, 33 * kb + 17)
            sync.dma_start(out=woT_d[kb * 128:(kb + 1) * 128, :], in_=wocb[:]).then_inc(woc.h, 16)
        sync.wait_ge(woc.h, 33 * 8)
        sync.wait_ge(upd.h, 1)
        for ft in range(32):
            sync.dma_start(out=vA[:, ft * W2:(ft + 1) * W2],
                           in_=vdn0_d[ft * 128:(ft + 1) * 128, :]).then_inc(ld.h, 16)
        for sq in range(len(up_seq), nseq):
            if sq >= 8:
                sync.wait_ge(mm.h, sq - 7)
            base = (sq - len(up_seq)) * 128
            sync.dma_start(out=tbuf[:, (sq % 8) * 128:(sq % 8) * 128 + 128],
                           in_=dnt_d[base:base + 128, :]).then_inc(tl.h, 16)

    def do_fold(tensor, meta, seq_base, W, ev_base, rd0_is_A):
        mmcnt = seq_base
        for s, lst in enumerate(meta):
            rd = (vA if (s % 2 == 0) == rd0_is_A else vB)
            i = 0
            while i < len(lst):
                X = lst[i][0]
                j = i
                while j < len(lst) and lst[j][0] == X:
                    j += 1
                if X >= 2:
                    tensor.wait_ge(ev.h, ev_base + s * 32 + X - 1)
                for k in range(i, j):
                    sq = seq_base + sum(len(meta[t]) for t in range(s)) + k
                    tensor.wait_ge(tl.h, 16 * (sq + 1))
                    Y = lst[k][1]
                    pp = pacc if X % 2 == 0 else pacc2
                    inst = tensor.matmul(pp[:, 0:W],
                                         tbuf[:, (sq % 8) * 128:(sq % 8) * 128 + 128],
                                         rd[:, Y * W:(Y + 1) * W],
                                         start=(k == i), stop=(k == j - 1))
                inst.then_inc(mm.h, j - i)
                i = j
            tensor.wait_ge(ev.h, ev_base + (s + 1) * 32)

    @block.tensor
    def _(tensor):
        tensor.wait_ge(ld.h, 16 * 33)
        do_fold(tensor, up_meta, 0, W1, 0, True)
        for ft in range(32):
            tensor.wait_ge(fin.h, ft + 2)
            tensor.transpose(ptr2[:, 0:128], vfb[:], idb[:]).then_inc(trp.h, 1)
            tensor.wait_ge(ev.h, NSTAGE * 32 + ft + 1)
        tensor.wait_ge(ld.h, 16 * 65)
        do_fold(tensor, dn_meta, len(up_seq), W2, (NSTAGE + 1) * 32, True)

    @block.scalar
    def _(scalar):
        for s in range(NSTAGE):
            wr = vB if s % 2 == 0 else vA
            for X in range(32):
                cnt = sum(1 for (xx, yy) in up_meta[s] if xx <= X)
                scalar.wait_ge(mm.h, sum(len(up_meta[t]) for t in range(s)) + cnt)
                pp = pacc if X % 2 == 0 else pacc2
                scalar.activation(wr[:, X * W1:(X + 1) * W1], pp[:, 0:W1],
                                  ACT.Copy).then_inc(ev.h, 1)
        for ft in range(32):
            scalar.wait_ge(trp.h, ft + 1)
            scalar.activation(outP[:, ft * 128:(ft + 1) * 128], ptr2[:, 0:128],
                              ACT.Copy).then_inc(ev.h, 1)
        for s_i in range(NSTAGE):
            wr = vB if s_i % 2 == 0 else vA
            last = (s_i == NSTAGE - 1)
            for X in range(32):
                cnt = sum(1 for (xx, yy) in dn_meta[s_i] if xx <= X)
                scalar.wait_ge(mm.h, len(up_seq) +
                               sum(len(dn_meta[t]) for t in range(s_i)) + cnt)
                pp = pacc if X % 2 == 0 else pacc2
                dstt = outC if last else wr
                Wd = W2
                scalar.activation(dstt[:, X * Wd:(X + 1) * Wd], pp[:, 0:W2],
                                  ACT.Copy).then_inc(ev.h, 1)

    @block.vector
    def _(vector):
        vector.wait_ge(ev.h, NSTAGE * 32)
        if W1 == 129:
            vector.tensor_copy(bcol_sb[:], vA[:, 128::W1]).then_inc(fin.h, 1)
        else:
            vector.sem_inc(fin.h, 1)
        for ft in range(32):
            if ft > 0:
                vector.wait_ge(trp.h, ft)
            vector.tensor_copy(vfb[:], vA[:, ft * W1:ft * W1 + 128]).then_inc(fin.h, 1)
        for kb in range(8):
            vector.wait_ge(woc.h, 33 * kb + 16)
            vector.tensor_copy(wocb[:], wocf[:]).then_inc(woc.h, 1)

    @block.gpsimd
    def _(gpsimd):
        gpsimd.wait_ge(fin.h, 1)
        gpsimd.dma_start(out=bcol_d, in_=bcol_sb[:]).then_inc(dst.h, 16)
        gpsimd.wait_ge(ev.h, NSTAGE * 32 + 32)
        gpsimd.dma_start(out=wexpP_d, in_=outP[:]).then_inc(dst.h, 16)
        gpsimd.wait_ge(fin.h, 33)
        gpsimd.wait_ge(dst.h, 32)
        gpsimd.sem_inc(upd.h, 1)
        gpsimd.wait_ge(ev.h, (NSTAGE + 1) * 32 + NSTAGE * 32)
        for ft in range(32):
            gpsimd.dma_start(out=wconP_d[ft * 128:(ft + 1) * 128, :],
                             in_=outC[:, ft * W2:(ft + 1) * W2]).then_inc(dst.h, 16)
        gpsimd.wait_ge(dst.h, 16 * 34)

    es.close()
    return nc


def host_L3a_inputs(core, WexpT, Wcon, b_exp, up_packs, dn_packs, Wo):
    sl = slice(core * 128, (core + 1) * 128)
    vup0 = np.concatenate([WexpT[:, sl], b_exp[:, None]], axis=1).astype(np.float32)
    vdn0 = np.ascontiguousarray(Wcon[:, sl]).astype(np.float32)
    upt = np.concatenate([t for lst in up_packs for t in lst], axis=0)
    dnt = np.concatenate([t for lst in dn_packs for t in lst], axis=0)
    ident = np.eye(128, dtype=np.float32).astype(BF)
    woTf = np.ascontiguousarray(Wo.T).astype(np.float32)
    return {"vup0": vup0, "vdn0": vdn0, "uptiles": upt, "dntiles": dnt,
            "identb": ident, "woTf": woTf}



def build_L3b(T=2048, serialize_gelu=False):
    TB = 512
    nb = T // TB
    nc = bass.Bass("TRN2", target_bir_lowering=False, detect_race_conditions=False)
    x_d = nc.dram_tensor("x", [T, D], F32, kind="ExternalInput").ap()
    xnT_d = nc.dram_tensor("xnT", [128, 8 * T], BF16, kind="ExternalInput").ap()
    ctxT_d = nc.dram_tensor("ctxT", [D, T], BF16, kind="ExternalInput").ap()
    wexpP_d = nc.dram_tensor("wexpP", [D, DHID], BF16, kind="ExternalInput").ap()
    wconP_d = nc.dram_tensor("wconP", [DHID, D], BF16, kind="ExternalInput").ap()
    woT_d = nc.dram_tensor("woT", [D, D], BF16, kind="ExternalInput").ap()
    bcol_d = nc.dram_tensor("bcol", [128, 32], F32, kind="ExternalInput").ap()
    upb_d = nc.dram_tensor("upb", [128, 32], F32, kind="ExternalInput").ap()
    bvecs_d = nc.dram_tensor("bvecs", [1, 3 * D], F32, kind="ExternalInput").ap()
    ones1_d = nc.dram_tensor("ones1", [1, 128], F32, kind="ExternalInput").ap()
    out_d = nc.dram_tensor("out", [T, D], F32, kind="ExternalOutput").ap()

    es = ExitStack()
    wexp = es.enter_context(nc.sbuf_tensor("wexp", [128, 8 * DHID], BF16))
    woT = es.enter_context(nc.sbuf_tensor("woT_sb", [128, 8 * D], BF16))
    ctxT = es.enter_context(nc.sbuf_tensor("ctxT_sb", [128, 8 * T], BF16))
    xnS = es.enter_context(nc.sbuf_tensor("xnS", [128, 2 * 8 * TB], BF16))
    h2T = es.enter_context(nc.sbuf_tensor("h2T", [128, 32 * TB], BF16))
    wcb = es.enter_context(nc.sbuf_tensor("wcb", [128, 2 * D], BF16))
    bp = es.enter_context(nc.sbuf_tensor("bp_sb", [128, 32], F32))
    upb = es.enter_context(nc.sbuf_tensor("upb_sb", [128, 32], F32))
    bvs = es.enter_context(nc.sbuf_tensor("bvs", [1, 3 * D], F32))
    bsum = es.enter_context(nc.sbuf_tensor("bsum", [1, D], F32))
    ones1 = es.enter_context(nc.sbuf_tensor("ones1_sb", [1, 128], F32))
    brep = es.enter_context(nc.sbuf_tensor("brep", [128, D], F32))
    xb = es.enter_context(nc.sbuf_tensor("xb", [128, D], F32))
    og = es.enter_context(nc.sbuf_tensor("og", [128, D], F32))
    pw = [es.enter_context(nc.psum_tensor(f"pw{i}", [128, 512], F32)) for i in range(8)]
    ld = S(es, nc, "ld")
    ldx = S(es, nc, "ldx")
    ldn = S(es, nc, "ldn")
    ldw = S(es, nc, "ldw")
    mm1 = S(es, nc, "mm1")
    gel = S(es, nc, "gel")
    mm2 = S(es, nc, "mm2")
    vev = S(es, nc, "vev")
    dst = S(es, nc, "dst")
    prep = S(es, nc, "prep")
    block = es.enter_context(nc.Block())

    def mm2_after(n):
        tb2, fk2 = divmod(n, 32)
        return tb2 * 40 + fk2 + 1

    @block.sync
    def _(sync):
        sync.dma_start(out=ones1[:], in_=ones1_d).then_inc(ld.h, 16)
        sync.dma_start(out=bvs[:], in_=bvecs_d).then_inc(ld.h, 16)
        sync.dma_start(out=bp[:], in_=bcol_d).then_inc(ld.h, 16)
        sync.dma_start(out=upb[:], in_=upb_d).then_inc(ld.h, 16)
        for kd in range(8):
            sync.dma_start(out=wexp[:, kd * DHID:(kd + 1) * DHID],
                           in_=wexpP_d[kd * 128:(kd + 1) * 128, :]).then_inc(ld.h, 16)
        for hk in range(8):
            sync.dma_start(out=woT[:, hk * D:(hk + 1) * D],
                           in_=woT_d[hk * 128:(hk + 1) * 128, :]).then_inc(ld.h, 16)
        for hk in range(8):
            sync.dma_start(out=ctxT[:, hk * T:(hk + 1) * T],
                           in_=ctxT_d[hk * 128:(hk + 1) * 128, :]).then_inc(ld.h, 16)
        for tb in range(nb):
            if tb >= 2:
                sync.wait_ge(mm1.h, 1 + (tb - 1) * 32)
            for kd in range(8):
                sync.dma_start(
                    out=xnS[:, (tb % 2) * 8 * TB + kd * TB:(tb % 2) * 8 * TB + (kd + 1) * TB],
                    in_=xnT_d[:, kd * T + tb * TB: kd * T + tb * TB + TB]).then_inc(ldn.h, 16)
            for fk in range(32):
                n = tb * 32 + fk
                if n >= 2:
                    sync.wait_ge(mm2.h, mm2_after(n - 2))
                sync.dma_start(out=wcb[:, (n % 2) * D:(n % 2) * D + D],
                               in_=wconP_d[fk * 128:(fk + 1) * 128, :]).then_inc(ldw.h, 16)

    @block.gpsimd
    def _(gpsimd):
        for tb in range(nb):
            for ts in range(4):
                i = tb * 4 + ts
                if i >= 1:
                    gpsimd.wait_ge(vev.h, 2 * i)
                gpsimd.dma_start(out=xb[:], in_=x_d[i * 128:(i + 1) * 128, :]).then_inc(ldx.h, 16)
                for oh in range(2):
                    gpsimd.wait_ge(vev.h, 2 * i + oh + 1)
                    gpsimd.dma_start(out=out_d[i * 128:(i + 1) * 128, oh * 512:(oh + 1) * 512],
                                     in_=og[:, oh * 512:(oh + 1) * 512]).then_inc(dst.h, 16)
        gpsimd.wait_ge(dst.h, 16 * nb * 4 * 2)

    @block.vector
    def _(vector):
        vector.sem_inc(prep.h, 8)
        vector.wait_ge(ld.h, 16 * 4)
        vector.tensor_add(bp[:], bp[:], upb[:])
        vector.tensor_add(bsum[:], bvs[:, 0:D], bvs[:, D:2 * D])
        vector.tensor_add(bsum[:], bsum[:], bvs[:, 2 * D:3 * D]).then_inc(prep.h, 1)
        for tb in range(nb):
            for ts in range(4):
                i = tb * 4 + ts
                vector.wait_ge(ldx.h, 16 * (i + 1))
                vector.wait_ge(prep.h, 10)
                if i >= 1:
                    vector.wait_ge(dst.h, 16 * 2 * i)
                vector.tensor_add(xb[:], xb[:], brep[:])
                vector.wait_ge(mm2.h, (tb + 1) * 40)
                for oh in range(2):
                    p = pw[ts * 2 + oh]
                    vector.scalar_tensor_tensor(og[:, oh * 512:(oh + 1) * 512],
                                                p[:], 1.0, xb[:, oh * 512:(oh + 1) * 512],
                                                ALU.mult, ALU.add).then_inc(vev.h, 1)

    @block.scalar
    def _(scalar):
        scalar.wait_ge(mm1.h, 1)
        scalar.activation(brep[:, 0:512], pw[6][:], ACT.Copy)
        scalar.activation(brep[:, 512:1024], pw[7][:], ACT.Copy).then_inc(prep.h, 1)
        scalar.wait_ge(prep.h, 9)
        for tb in range(nb):
            for fk in range(32):
                n = tb * 32 + fk
                scalar.wait_ge(mm1.h, 1 + n + 1)
                p = pw[fk % 2]
                scalar.activation(h2T[:, fk * TB:(fk + 1) * TB], p[:],
                                  ACT.Gelu, bias=bp[:, fk:fk + 1]).then_inc(gel.h, 1)

    @block.tensor
    def _(tensor):
        tensor.wait_ge(prep.h, 9)
        tensor.matmul(pw[6][:], ones1[:], bsum[:, 0:512], start=True, stop=True)
        tensor.matmul(pw[7][:], ones1[:], bsum[:, 512:1024], start=True,
                      stop=True).then_inc(mm1.h, 1)
        tensor.wait_ge(ld.h, 16 * 28)
        for tb in range(nb):
            tensor.wait_ge(ldn.h, 16 * 8 * (tb + 1))
            if tb > 0:
                tensor.wait_ge(vev.h, tb * 8)
            for fk in range(32):
                n = tb * 32 + fk
                if serialize_gelu:
                    tensor.wait_ge(gel.h, n)
                elif n >= 2:
                    tensor.wait_ge(gel.h, n - 1)
                p = pw[fk % 2]
                for kd in range(8):
                    inst = tensor.matmul(
                        p[:], wexp[:, kd * DHID + fk * 128: kd * DHID + (fk + 1) * 128],
                        xnS[:, (tb % 2) * 8 * TB + kd * TB:(tb % 2) * 8 * TB + (kd + 1) * TB],
                        start=(kd == 0), stop=(kd == 7))
                inst.then_inc(mm1.h, 1)
            tensor.wait_ge(gel.h, (tb + 1) * 32)
            if tb == 0:
                tensor.wait_ge(prep.h, 10)
            for fk in range(32):
                nn = tb * 32 + fk
                tensor.wait_ge(ldw.h, 16 * (nn + 1))
                for ts in range(4):
                    tensor.matmul(pw[ts * 2 + 0][:],
                                  h2T[:, fk * TB + ts * 128: fk * TB + (ts + 1) * 128],
                                  wcb[:, (nn % 2) * D: (nn % 2) * D + 512],
                                  start=(fk == 0), stop=False)
                    inst = tensor.matmul(pw[ts * 2 + 1][:],
                                         h2T[:, fk * TB + ts * 128: fk * TB + (ts + 1) * 128],
                                         wcb[:, (nn % 2) * D + 512: (nn % 2) * D + 1024],
                                         start=(fk == 0), stop=False)
                inst.then_inc(mm2.h, 1)
            for hk in range(8):
                for ts in range(4):
                    t0 = tb * TB + ts * 128
                    tensor.matmul(pw[ts * 2 + 0][:],
                                  ctxT[:, hk * T + t0: hk * T + t0 + 128],
                                  woT[:, hk * D: hk * D + 512],
                                  start=False, stop=(hk == 7))
                    inst = tensor.matmul(pw[ts * 2 + 1][:],
                                         ctxT[:, hk * T + t0: hk * T + t0 + 128],
                                         woT[:, hk * D + 512: hk * D + 1024],
                                         start=False, stop=(hk == 7))
                inst.then_inc(mm2.h, 1)

    es.close()
    return nc



C2 = 32
SUP = 2048
NCH2 = SUP // C2
RPITCH = 2 * C2 + 4
CW2 = 1 + C2 * (RPITCH + 1)


class _Marks:
    def __init__(self):
        self.cnt = {}
        self.tab = {}

    def inc(self, sem, n):
        self.cnt[sem] = self.cnt.get(sem, 0) + n

    def mark(self, key, sem):
        self.tab[key] = (sem, self.cnt.get(sem, 0))


class _Dummy:
    def then_inc(self, *a, **k):
        return self


class _Eng:

    def __init__(self, eng, name, marks, real, sems):
        self.e = eng
        self.n = name
        self.m = marks
        self.real = real
        self.sems = sems

    def __getattr__(self, op):
        if op in ("e", "n", "m", "real", "sems"):
            raise AttributeError(op)
        proxy = self

        def call(*a, **k):
            if proxy.real:
                inst = getattr(proxy.e, op)(*a, **k)

                class W:
                    def then_inc(self, semname, nn):
                        proxy.m.inc(semname, nn)
                        inst.then_inc(proxy.sems[semname], nn)
                        return self
                return W()
            else:
                class W2:
                    def then_inc(self, semname, nn):
                        proxy.m.inc(semname, nn)
                        return self
                return W2()
        return call

    def wait(self, key):
        semname, val = self.m.tab.get(key, (None, None)) if not self.real \
            else self.m.tab[key]
        if self.real:
            if val > 0:
                self.e.wait_ge(self.sems[semname], val)

    def mark(self, key, semname):
        self.m.mark(key, semname)


def build_L2(L=16384):
    import os as _os
    ST = set(_os.environ.get("L2STUB", "").split(",")) - {""}
    nsup = L // SUP
    NBLK = SUP // 1024
    nc = bass.Bass("TRN2", target_bir_lowering=False, detect_race_conditions=False)
    xnT_d = nc.dram_tensor("xnTf", [128, 8 * L], BF16, kind="ExternalInput").ap()
    wcat_d = nc.dram_tensor("wcat", [D, 768], F32, kind="ExternalInput").ap()
    prm2_d = nc.dram_tensor("prm2", [128, 2], F32, kind="ExternalInput").ap()
    nwc_d = nc.dram_tensor("nwc", [128, 1], F32, kind="ExternalInput").ap()
    c_ob_d = nc.dram_tensor("c_ob", [128, 128], F32, kind="ExternalInput").ap()
    c_obn_d = nc.dram_tensor("c_obn", [128, 128], F32, kind="ExternalInput").ap()
    c_os_d = nc.dram_tensor("c_os", [128, 128], F32, kind="ExternalInput").ap()
    c_osn_d = nc.dram_tensor("c_osn", [128, 128], F32, kind="ExternalInput").ap()
    c_os2_d = nc.dram_tensor("c_os2", [128, 64], F32, kind="ExternalInput").ap()
    c_o2_d = nc.dram_tensor("c_o2", [128, 2], F32, kind="ExternalInput").ap()
    c_mn_d = nc.dram_tensor("c_mn", [64, 32], F32, kind="ExternalInput").ap()
    idf_d = nc.dram_tensor("idf32", [128, 128], F32, kind="ExternalInput").ap()
    ctxT_d = nc.dram_tensor("ctxT", [128, L], BF16, kind="ExternalOutput").ap()
    sst0_d = nc.dram_tensor("sst0", [128, 128], F32, kind="ExternalInput").ap()
    sstf_d = nc.dram_tensor("sstf", [128, 128], F32, kind="ExternalOutput").ap()
    DBG = _os.environ.get("L2DBG", "")
    if DBG:
        dbgys_d = nc.dram_tensor("dbgys", [128, L], F32, kind="ExternalOutput").ap()

    es = ExitStack()
    sb = lambda n, s, dt=F32: es.enter_context(nc.sbuf_tensor(n, s, dt))
    wc = sb("wc", [128, 8 * 768], BF16)
    wcf = sb("wcf", [128, 768], F32)
    nwc = sb("nwc_sb", [128, 1])
    c_ob = sb("c_ob_sb", [128, 128])
    c_obn = sb("c_obn_sb", [128, 128])
    c_os = sb("c_os_sb", [128, 128])
    c_osn = sb("c_osn_sb", [128, 128])
    c_os2 = sb("c_os2_sb", [128, 64])
    c_o2 = sb("c_o2_sb", [128, 2])
    c_mn = sb("c_mn_sb", [64, 32])
    idf = sb("idf_sb", [128, 128])
    prm2 = sb("prm2_sb", [128, 2])
    expA = sb("expA", [128, 1])
    xnS = sb("xnS2", [128, 2 * 8 * 1024], BF16)
    qT = sb("qT", [128, SUP])
    kT = sb("kT", [128, SUP])
    vTn = sb("vTn", [128, SUP])
    gT = sb("gT", [128, SUP])
    spl = sb("spl", [128, SUP])
    btT = sb("btT", [128, SUP])
    rbA = sb("rbA", [128, SUP])
    rbB = sb("rbB", [128, SUP])
    rbC = sb("rbC", [128, SUP])
    apT = rbA[0:2, :]
    laT = rbA[32:34, :]
    clsR = rbA[64:66, :]
    r_kg = rbA[96:98, :]
    rowB = rbB[0:2, :]
    r_gam = rbB[32:34, :]
    rowA = rbB[64:66, :]
    r_kp = rbB[96:98, :]
    r_c2 = rbC[0:2, :]
    r_cB = rbC[32:34, :]
    r_qh = rbC[64:66, :]
    r_cc = rbC[96:98, 0:NCH2]
    sqT = sb("sqT", [128, SUP])
    coeff = sb("coeff", [128, CW2])
    work = sb("work", [128, 2 * C2])
    trash = sb("trash", [128, 2 * C2])
    kpr = sb("kpr", [128, C2 * C2])
    cdr = sb("cdr", [128, C2 * C2])
    xrep = sb("xrep", [128, C2 * C2])
    cBr = sb("cBr", [128, C2])
    c2r = sb("c2r", [128, C2])
    gmr = sb("gmr", [128, C2])
    ktil = sb("ktil", [128, C2])
    kprm = sb("kprm", [128, C2])
    qhat = sb("qhat", [128, C2])
    urm = sb("urm", [32, 128])
    krm = sb("krm", [32, 128])
    dstA = sb("dstA", [64, 32])
    c2t = sb("c2t", [64, 2])
    dqk = sb("dqk", [32, 64])
    Sst = sb("Sst", [128, 128])
    ysS = sb("ysS", [128, C2])
    ysqv = sb("ysqv", [128, C2])
    eg = sb("eg", [128, 2 * C2])
    den = sb("den", [128, C2])
    rec = sb("rec", [128, C2])
    g2v = sb("g2v", [128, C2])
    rsr = sb("rsr", [2, 32])
    rsq = sb("rsq", [128, C2])
    qh2 = sb("qh2", [128, C2])
    c2pad = sb("c2pad", [2, 64])
    zro = sb("zro", [128, C2])
    dbgt = sb("dbgt", [128, C2])
    spc = sb("spc", [128, 8])
    spr = sb("spr", [2, 8])
    ctxS = sb("ctxS", [128, SUP], BF16)

    pPJa = es.enter_context(nc.psum_tensor("pPJa", [128, 512], F32))
    pPJb = es.enter_context(nc.psum_tensor("pPJb", [128, 512], F32))
    pREP = es.enter_context(nc.psum_tensor("pREP", [128, 512], F32))
    pGRAM = es.enter_context(nc.psum_tensor("pGRAM", [128, 1024], F32))
    pWQ = es.enter_context(nc.psum_tensor("pWQ", [128, 512], F32))
    pUY = es.enter_context(nc.psum_tensor("pUY", [128, 256], F32))
    pYb = es.enter_context(nc.psum_tensor("pYb", [128, 128], F32))
    pU = pUY[:, 0:256]
    pY = pYb[:, 0:32]
    pRS1 = pYb[0:2, 32:64]
    pRSREP = pYb[:, 64:96]
    pS = pPJa[:, 0:128]

    semnames = ["ld", "lx", "tn", "vn", "sn", "gn", "dst"]
    sems = {n: es.enter_context(nc.semaphore(n)) for n in semnames}
    block = es.enter_context(nc.Block())

    marks = _Marks()

    def em_sync(E):
        for i, (s_, d_) in enumerate([(nwc, nwc_d), (c_ob, c_ob_d), (c_obn, c_obn_d),
                                      (c_os, c_os_d), (c_osn, c_osn_d), (c_os2, c_os2_d),
                                      (c_o2, c_o2_d), (c_mn, c_mn_d), (idf, idf_d),
                                      (prm2, prm2_d)]):
            E.dma_start(out=s_[:], in_=d_).then_inc("ld", 16)
        E.dma_start(out=Sst[:], in_=sst0_d).then_inc("ld", 16)
        E.mark(("consts",), "ld")
        for kd in range(8):
            if kd > 0:
                E.wait(("wcast", kd - 1))
            E.dma_start(out=wcf[:], in_=wcat_d[kd * 128:(kd + 1) * 128, :]).then_inc("ld", 16)
            E.mark(("wcf", kd), "ld")
        for sp in range(nsup):
            for b in range(NBLK):
                ib = sp * NBLK + b
                if ib >= 2:
                    E.wait(("proj", ib - 2))
                for kd in range(8):
                    E.dma_start(
                        out=xnS[:, (ib % 2) * 8192 + kd * 1024:(ib % 2) * 8192 + (kd + 1) * 1024],
                        in_=xnT_d[:, kd * L + ib * 1024: kd * L + ib * 1024 + 1024]
                    ).then_inc("lx", 16)
                E.mark(("xn", ib), "lx")
        SSTEXP = True
        for sp in range(nsup):
            if DBG:
                for j in range(NCH2):
                    g = sp * NCH2 + j
                    if DBG == "ys":
                        E.wait(("ysq", g))
                        E.dma_start(out=dbgys_d[:, g * C2:(g + 1) * C2],
                                    in_=ysS[:]).then_inc("dst", 16)
                    elif DBG == "rs":
                        E.wait(("dbgv", g))
                        E.dma_start(out=dbgys_d[:, g * C2:(g + 1) * C2],
                                    in_=dbgt[:]).then_inc("dst", 16)
                    elif DBG in ("r1", "r2"):
                        E.wait(("dbgv", g))
                        E.dma_start(out=dbgys_d[:, g * C2:(g + 1) * C2],
                                    in_=dbgt[:]).then_inc("dst", 16)
                    elif DBG == "g2":
                        E.wait(("g2", g))
                        E.dma_start(out=dbgys_d[:, g * C2:(g + 1) * C2],
                                    in_=g2v[:]).then_inc("dst", 16)
                    E.mark(("dbgd", g), "dst")
            E.wait(("ctx", sp * NCH2 + NCH2 - 1))
            E.dma_start(out=ctxT_d[:, sp * SUP:(sp + 1) * SUP], in_=ctxS[:]).then_inc("dst", 16)
            E.mark(("ctxdma", sp), "dst")
        E.wait(("alldone",))
        E.dma_start(out=sstf_d, in_=Sst[:]).then_inc("dst", 16)
        if E.real:
            E.e.wait_ge(sems["dst"], 16 * (nsup + 1) + 16 * nsup * (NCH2 if DBG else 0))

    def em_pe(E):
        for sp in range(nsup):
            for b in range(NBLK):
                ib = sp * NBLK + b
                E.wait(("xn", ib))
                E.wait(("wcast", 7))
                if b == 0 and sp > 0:
                    E.wait(("ctx", sp * NCH2 - 1))
                if ib > 0:
                    E.wait(("pjev", ib - 1))
                for m in range(6):
                    if m > 0:
                        E.wait(("pjev_m", ib, m - 1))
                    Nw = 128 if m < 5 else 2
                    if "P" in ST:
                        E.matmul(pWQ[0:2, 448:480], idf[0:2, 0:2], idf[0:2, 0:32],
                                 start=True, stop=True)
                    else:
                        for kd in range(8):
                            for hf in range(2):
                                pp = pPJa if hf == 0 else pPJb
                                E.matmul(pp[0:Nw, :],
                                         wc[:, kd * 768 + m * 128: kd * 768 + m * 128 + Nw],
                                         xnS[:, (ib % 2) * 8192 + kd * 1024 + hf * 512:
                                             (ib % 2) * 8192 + kd * 1024 + hf * 512 + 512],
                                         start=(kd == 0), stop=(kd == 7))
                    E.drain().then_inc("tn", 1)
                    E.mark(("pjmm", ib, m), "tn")
                E.mark(("proj", ib), "tn")
            E.wait(("sqk", sp))
            for i4 in range(4):
                if i4 > 0:
                    E.wait(("ksqev", sp, i4 - 1))
                if "C" in ST or "c_ksq" in ST:
                    E.matmul(pWQ[0:2, 448:480], idf[0:2, 0:2], idf[0:2, 0:32],
                             start=True, stop=True)
                else:
                    E.matmul(pREP[0:2, 0:512], c_o2[:],
                             sqT[:, i4 * 512:(i4 + 1) * 512], start=True, stop=True)
                E.drain().then_inc("tn", 1)
                E.mark(("ksq", sp, i4), "tn")
            E.wait(("sqq", sp))
            for i4 in range(4):
                E.wait(("ksqev", sp, 3) if i4 == 0 else ("qsqev", sp, i4 - 1))
                if i4 == 0:
                    E.wait(("latc", sp))
                if "C" in ST or "c_qsq" in ST:
                    E.matmul(pWQ[0:2, 448:480], idf[0:2, 0:2], idf[0:2, 0:32],
                             start=True, stop=True)
                else:
                    E.matmul(pREP[0:2, 0:512], c_o2[:],
                             sqT[:, i4 * 512:(i4 + 1) * 512], start=True, stop=True)
                E.drain().then_inc("tn", 1)
                E.mark(("qsq", sp, i4), "tn")
            for j in range(NCH2):
                g = sp * NCH2 + j
                t0 = j * C2
                if g > 0:
                    E.wait(("vprep1", g - 1))
                E.wait(("rows_s", sp))
                if sp == 0 and j == 0:
                    E.wait(("init",))
                if not ("C" in ST or "c_rep" in ST):
                    rowbase = {0: 32, 1: 96, 2: 96, 3: 64, 5: 32, 6: 0}
                    prev_bse = 0
                    for idx, row in [(6, r_c2), (0, r_gam), (5, r_cB), (3, r_qh),
                                     (1, r_kg), (2, r_kp)]:
                        bse = rowbase[idx]
                        if bse != prev_bse:
                            E.drain()
                        prev_bse = bse
                        src_sel = c_osn if row is r_kp else c_os
                        kw = {}
                        if bse == 96:
                            kw["tile_position"] = (96, 0)
                        E.matmul(pREP[:, idx * 32:(idx + 1) * 32],
                                 src_sel[bse:bse + 2, :],
                                 row[:, t0:t0 + C2], start=True, stop=True, **kw)
                    E.drain()
                else:
                    E.matmul(pWQ[0:2, 448:480], idf[0:2, 0:2], idf[0:2, 0:32],
                             start=True, stop=True)
                E.drain().then_inc("tn", 1)
                E.mark(("rep", g), "tn")
                E.wait(("kpr", g))
                if g > 0:
                    E.wait(("coeff", g - 1))
                if not ("C" in ST or "c_gram" in ST):
                    E.matmul(pGRAM[:, 0:512], c_obn[:], kpr[:, 0:512], start=True, stop=True)
                    E.matmul(pGRAM[:, 512:1024], c_obn[:], kpr[:, 512:1024], start=True, stop=True)
                else:
                    E.matmul(pWQ[0:2, 448:480], idf[0:2, 0:2], idf[0:2, 0:32],
                             start=True, stop=True)
                E.drain().then_inc("tn", 1)
                E.mark(("gram", g), "tn")
                E.wait(("vprep1", g))
                if not ("C" in ST or "c_qkt" in ST):
                    E.matmul(pWQ[0:32, 32:64], kT[0:64, t0:t0 + C2], qhat[0:64, :],
                             start=True, stop=True, tile_position=(0, 0))
                    E.transpose(pWQ[0:32, 64:66], r_c2[:, t0:t0 + C2], idf[0:2, 0:2])
                    E.transpose(pWQ[0:64, 66:68], c2pad[:], idf[0:2, 0:2])
                    E.transpose(pU[0:32, 128:256], kprm[:], idf[:])
                    E.drain()
                    E.matmul(pWQ[32:64, 32:64], kT[64:128, t0:t0 + C2], qhat[64:128, :],
                             start=True, stop=True, tile_position=(64, 32))
                    E.matmul(pWQ[0:64, 160:192], c_os2[64:66, :], clsR[:, t0:t0 + C2],
                             start=True, stop=True)
                    E.drain()
                else:
                    E.matmul(pWQ[0:2, 448:480], idf[0:2, 0:2], idf[0:2, 0:32],
                             start=True, stop=True)
                E.drain().then_inc("tn", 1)
                E.mark(("qkt", g), "tn")
                if g > 0:
                    E.wait(("ysq", g - 1))
                if not ("C" in ST or "c_wy1" in ST):
                    E.matmul(pWQ[:, 0:32], Sst[:], ktil[:], start=True, stop=True)
                    E.matmul(pY[:], Sst[:], qh2[:], start=True, stop=True)
                else:
                    E.matmul(pWQ[0:2, 448:480], idf[0:2, 0:2], idf[0:2, 0:32],
                             start=True, stop=True)
                E.drain().then_inc("tn", 1)
                E.mark(("wy1", g), "tn")
                E.wait(("solve", g))
                if not ("C" in ST or "c_utr" in ST):
                    E.transpose(pU[0:32, 0:128], work[:, C2:2 * C2], idf[:])
                else:
                    E.matmul(pWQ[0:2, 448:480], idf[0:2, 0:2], idf[0:2, 0:32],
                             start=True, stop=True)
                E.drain().then_inc("tn", 1)
                E.mark(("utr", g), "tn")
                E.wait(("uev", g))
                E.wait(("dqk", g))
                if not ("C" in ST or "c_y2s" in ST):
                    E.matmul(pY[0:64, :], urm[:, 0:64], dqk[:, 0:32], start=False, stop=True,
                             tile_position=(0, 0))
                    E.matmul(pY[64:128, :], urm[:, 64:128], dqk[:, 32:64], start=False, stop=True,
                             tile_position=(0, 64))
                    E.matmul(pS[:], krm[:], urm[:], start=True, stop=True)
                else:
                    E.matmul(pWQ[0:2, 448:480], idf[0:2, 0:2], idf[0:2, 0:32],
                             start=True, stop=True)
                E.drain().then_inc("tn", 1)
                E.mark(("y2s", g), "tn")
                E.wait(("ysq", g))
                if not ("C" in ST or "c_rms1" in ST):
                    E.matmul(pRSREP[:], c_ob[:], ysqv[:], start=True, stop=True)
                else:
                    E.matmul(pWQ[0:2, 448:480], idf[0:2, 0:2], idf[0:2, 0:32],
                             start=True, stop=True)
                E.drain().then_inc("tn", 1)
                E.mark(("rms1", g), "tn")

    def em_gp(E):
        for sp in range(nsup):
            for j in range(NCH2):
                g = sp * NCH2 + j
                t0 = j * C2
                E.wait(("pjev", sp * NBLK + NBLK - 1))
                if g > 0:
                    E.wait(("gram", g - 1))
                if "G" not in ST:
                    E.tensor_tensor(
                        kpr[:].rearrange("p (t s) -> p t s", s=C2),
                        kT[:, t0:t0 + C2].rearrange("p (t one) -> p t one", one=1).broadcast_to([128, C2, C2]),
                        kT[:, t0:t0 + C2].rearrange("p (one s) -> p one s", one=1).broadcast_to([128, C2, C2]),
                        ALU.mult)
                E.tensor_copy(spc[:, 2:3], kpr[:, 0:1]).then_inc("gn", 1)
                E.mark(("kpr", g), "gn")
                E.wait(("cbr", g))
                if g > 0:
                    E.wait(("xrep", g - 1))
                if "G" not in ST:
                    E.tensor_tensor(
                        cdr[:].rearrange("p (t s) -> p t s", s=C2),
                        cBr[:].rearrange("p (t one) -> p t one", one=1).broadcast_to([128, C2, C2]),
                        c2r[:].rearrange("p (one s) -> p one s", one=1).broadcast_to([128, C2, C2]),
                        ALU.subtract)
                E.tensor_copy(spc[:, 3:4], cdr[:, 0:1]).then_inc("gn", 1)
                E.mark(("cdr", g), "gn")

    def em_s(E):
        E.wait(("consts",))
        E.activation(expA[0:2, 0:1], prm2[0:2, 1:2], ACT.Exp)
        E.activation(spr[0:2, 0:1], expA[0:2, 0:1], ACT.Copy).then_inc("sn", 1)
        E.mark(("expA",), "sn")
        for sp in range(nsup):
            if sp > 0:
                E.wait(("cdr", sp * NCH2 - 1))
                E.wait(("ctx", sp * NCH2 - 1))
            for b in range(NBLK):
                ib = sp * NBLK + b
                boff = b * 1024
                for m in range(6):
                    E.wait(("pjmm", ib, m))
                    if "P" not in ST:
                        for hf in range(2):
                            pp = pPJa if hf == 0 else pPJb
                            o = boff + hf * 512
                            if m == 0:
                                E.activation(qT[:, o:o + 512], pp[:], ACT.Copy)
                            elif m == 1:
                                E.activation(kT[:, o:o + 512], pp[:], ACT.Copy)
                            elif m == 2:
                                E.activation(vTn[:, o:o + 512], pp[:], ACT.Copy, scale=-1.0)
                            elif m == 3:
                                E.activation(btT[:, o:o + 512], pp[:], ACT.Exp, scale=-1.0)
                            elif m == 4:
                                E.activation(gT[:, o:o + 512], pp[:], ACT.Copy)
                            else:
                                E.activation(apT[:, o:o + 512], pp[0:2, :], ACT.Copy)
                    E.activation(spc[0:64, 4:5], spc[0:64, 3:4], ACT.Copy).then_inc("sn", 1)
                    E.mark(("pjev_m", ib, m), "sn")
                E.mark(("pjev", ib), "sn")
            if "R" not in ST:
                E.activation(spl[:, 0:SUP], btT[:, 0:SUP], ACT.Ln, bias=1.0)
                E.activation(btT[:, 0:SUP], spl[:, 0:SUP], ACT.Exp, scale=-1.0)
            E.wait(("vap", sp))
            if "R" not in ST:
                E.activation(rowB[:, 0:SUP], rowA[:, 0:SUP], ACT.Exp)
                E.activation(rowB[:, 0:SUP], rowB[:, 0:SUP], ACT.Ln, bias=1.0)
            E.activation(spr[0:2, 1:2], spr[0:2, 0:1], ACT.Copy).then_inc("sn", 1)
            E.mark(("sp_sp", sp), "sn")
            for i4 in range(4):
                E.wait(("ksq", sp, i4))
                if "R" not in ST:
                    E.activation(rowA[:, i4 * 512:(i4 + 1) * 512], pREP[0:2, 0:512], ACT.Ln)
                E.activation(spr[0:2, 2:3], spr[0:2, 0:1], ACT.Copy).then_inc("sn", 1)
                E.mark(("ksqev", sp, i4), "sn")
            E.mark(("lnk", sp), "sn")
            for i4 in range(4):
                E.wait(("qsq", sp, i4))
                if "R" not in ST:
                    E.activation(rowB[:, i4 * 512:(i4 + 1) * 512], pREP[0:2, 0:512], ACT.Ln)
                E.activation(spr[0:2, 3:4], spr[0:2, 0:1], ACT.Copy).then_inc("sn", 1)
                E.mark(("qsqev", sp, i4), "sn")
            E.mark(("lnq", sp), "sn")
            E.wait(("rowargs", sp))
            if "R" not in ST:
                E.activation(r_gam[:, 0:SUP], clsR[:, 0:SUP], ACT.Exp)
                E.activation(r_kg[:, 0:SUP], r_kg[:, 0:SUP], ACT.Exp)
                E.activation(r_kp[:, 0:SUP], r_kp[:, 0:SUP], ACT.Exp)
                E.activation(r_qh[:, 0:SUP], r_qh[:, 0:SUP], ACT.Exp)
            E.activation(spr[0:2, 4:5], spr[0:2, 0:1], ACT.Copy).then_inc("sn", 1)
            E.mark(("rows_s", sp), "sn")
            for j in range(NCH2):
                g = sp * NCH2 + j
                t0 = j * C2
                if j > 1:
                    E.wait(("g2", g - 2))
                if "S" not in ST:
                    E.activation(eg[:, (j % 2) * C2:(j % 2) * C2 + C2],
                                 gT[:, t0:t0 + C2], ACT.Exp, scale=-1.0)
                E.activation(spc[0:64, 4:5], spc[0:64, 3:4], ACT.Copy).then_inc("sn", 1)
                E.mark(("eg", g), "sn")
                E.wait(("cdr", g))
                if "S" not in ST:
                    E.activation(xrep[:], cdr[:], ACT.Exp)
                E.activation(spc[0:64, 4:5], spc[0:64, 3:4], ACT.Copy).then_inc("sn", 1)
                E.mark(("xrep", g), "sn")
                E.wait(("dsta", g))
                if "S" not in ST:
                    E.activation(dstA[:], dstA[:], ACT.Exp)
                E.activation(spc[0:32, 4:5], spc[0:32, 3:4], ACT.Copy).then_inc("sn", 1)
                E.mark(("dste", g), "sn")
                E.wait(("utr", g))
                if "S" not in ST:
                    E.activation(urm[:, 0:128], pU[0:32, 0:128], ACT.Copy)
                    E.activation(krm[:, 0:128], pU[0:32, 128:256], ACT.Copy)
                E.activation(spc[0:32, 4:5], spc[0:32, 3:4], ACT.Copy).then_inc("sn", 1)
                E.mark(("uev", g), "sn")
                E.wait(("rms1v", g))
                if "S" not in ST:
                    E.activation(rsq[:], rsq[:], ACT.Ln)
                    E.activation(spc[0:64, 4:5], spc[0:64, 3:4], ACT.Copy)
                    E.activation(rsq[:], rsq[:], ACT.Exp, scale=-0.5)
                    E.activation(spc[0:64, 4:5], spc[0:64, 3:4], ACT.Copy)
                E.activation(spr[0:2, 5:6], spr[0:2, 0:1], ACT.Copy).then_inc("sn", 1)
                E.mark(("rsr", g), "sn")

    def em_v(E):
        for kd in range(8):
            E.wait(("wcf", kd))
            E.tensor_copy(wc[:, kd * 768:(kd + 1) * 768], wcf[:]).then_inc("vn", 1)
            E.mark(("wcast", kd), "vn")
        E.memset(coeff[:], 0.0)
        E.memset(work[:], 0.0)
        E.memset(trash[:], 0.0)
        E.memset(zro[:], 0.0)
        E.memset(spc[:], 0.0).then_inc("vn", 1)
        E.mark(("init",), "vn")
        for sp in range(nsup):
            E.wait(("pjev", sp * NBLK + NBLK - 1))
            E.wait(("expA",))
            if "R" not in ST:
                E.tensor_scalar(rowA[:, 0:SUP], apT[:, 0:SUP], prm2[0:2, 0:1], None,
                                ALU.add)
            E.tensor_copy(spr[0:2, 6:7], spr[0:2, 0:1]).then_inc("vn", 1)
            E.mark(("vap", sp), "vn")
            if "R" not in ST:
                E.tensor_tensor(sqT[:], kT[:], kT[:], ALU.mult)
            E.tensor_copy(spc[:, 5:6], spc[:, 3:4]).then_inc("vn", 1)
            E.mark(("sqk", sp), "vn")
            E.wait(("lnk", sp))
            if "R" not in ST:
                E.tensor_tensor(sqT[:], qT[:], qT[:], ALU.mult)
            E.tensor_copy(spc[:, 5:6], spc[:, 3:4]).then_inc("vn", 1)
            E.mark(("sqq", sp), "vn")
            E.wait(("sp_sp", sp))
            if "R" not in ST:
                E.tensor_scalar(laT[:, 0:SUP], rowB[:, 0:SUP], expA[0:2, 0:1], None,
                                ALU.mult).then_inc("vn", 1)
            else:
                E.tensor_copy(spc[:, 5:6], spc[:, 3:4]).then_inc("vn", 1)
            E.mark(("latc", sp), "vn")
            if "R" not in ST:
                E.tensor_scalar(laT[:, 0:SUP], laT[:, 0:SUP], -1.0, None, ALU.mult)
            if "R" not in ST:
                for j in range(NCH2):
                    E.tensor_tensor_scan(clsR[:, j * C2:(j + 1) * C2], laT[:, j * C2:(j + 1) * C2],
                                         zro[32:34, :], 0.0, ALU.add, ALU.add)
                E.tensor_copy(spr[0:2, 7:8], clsR[0:2, 0:1])
                E.tensor_copy(r_cc[:], clsR[:, C2 - 1::C2])
            E.wait(("lnq", sp))
            if "R" not in ST:
                E.scalar_tensor_tensor(r_kg[:, 0:SUP], rowA[:, 0:SUP], -0.5,
                                       clsR[:, 0:SUP], ALU.mult, ALU.add)
                E.scalar_tensor_tensor(r_c2[:, 0:SUP], rowA[:, 0:SUP], 0.5,
                                       clsR[:, 0:SUP], ALU.mult, ALU.add)
                E.scalar_tensor_tensor(r_cB[:, 0:SUP], rowA[:, 0:SUP], -0.5,
                                       clsR[:, 0:SUP], ALU.mult, ALU.add)
                E.tensor_scalar(r_qh[:, 0:SUP], rowB[:, 0:SUP], -0.5, -2.0794415416798357,
                                ALU.mult, ALU.add)
                E.scalar_tensor_tensor(r_kp[:, 0:SUP], rowA[:, 0:SUP], -0.5,
                                       clsR[:, 0:SUP], ALU.mult, ALU.subtract)
                E.tensor_tensor(r_kp[:, 0:SUP].rearrange("p (c t) -> p c t", t=C2),
                                r_kp[:, 0:SUP].rearrange("p (c t) -> p c t", t=C2),
                                r_cc[:].rearrange("p (c one) -> p c one", one=1).broadcast_to([2, NCH2, C2]),
                                ALU.add)
            E.tensor_copy(spr[0:2, 6:7], spr[0:2, 0:1]).then_inc("vn", 1)
            E.mark(("rowargs", sp), "vn")
            for j in range(NCH2):
                g = sp * NCH2 + j
                t0 = j * C2
                E.wait(("rep", g))
                E.wait(("rows_s", sp))
                if "V" not in ST:
                    E.tensor_copy(gmr[:], pREP[:, 0:32])
                    if j == 0:
                        E.scalar_tensor_tensor(cBr[:], pREP[:, 5 * 32:6 * 32], 1.0,
                                               spl[:, t0:t0 + C2], ALU.mult, ALU.subtract)
                        E.tensor_copy(c2r[:], pREP[:, 6 * 32:7 * 32])
                if j == 0:
                    E.tensor_copy(spc[:, 6:7], spc[:, 3:4]).then_inc("vn", 1)
                    E.mark(("cbr", g), "vn")
                if "V" not in ST:
                    E.scalar_tensor_tensor(ktil[:], kT[:, t0:t0 + C2], 1.0,
                                           pREP[:, 1 * 32:2 * 32], ALU.mult, ALU.mult)
                    E.scalar_tensor_tensor(kprm[:], kT[:, t0:t0 + C2], 1.0,
                                           pREP[:, 2 * 32:3 * 32], ALU.mult, ALU.mult)
                    E.scalar_tensor_tensor(qhat[:], qT[:, t0:t0 + C2], 1.0,
                                           pREP[:, 3 * 32:4 * 32], ALU.mult, ALU.mult)
                    E.tensor_copy(c2pad[:, 32:64], r_c2[:, t0:t0 + C2])
                    E.tensor_tensor(qh2[:], qhat[:], gmr[:], ALU.mult)
                E.tensor_copy(spc[:, 5:6], spc[:, 3:4]).then_inc("vn", 1)
                E.mark(("vprep1", g), "vn")
                E.wait(("xrep", g))
                E.wait(("gram", g))
                if "V" not in ST:
                    cf3 = coeff[:, 1:1 + C2 * (RPITCH + 1)].rearrange("p (t s) -> p t s", s=RPITCH + 1)
                    E.scalar_tensor_tensor(cf3[:, :, 0:C2], xrep[:].rearrange("p (t s) -> p t s", s=C2),
                                           1.0, pGRAM[:].rearrange("p (t s) -> p t s", s=C2),
                                           ALU.mult, ALU.mult)
                    ones_pos = coeff[:, 0:C2 * RPITCH].rearrange("p (t s) -> p t s", s=RPITCH)
                    E.tensor_copy(ones_pos[:, :, 0:1],
                                  btT[:, t0:t0 + C2].rearrange("p (t one) -> p t one", one=1))
                E.tensor_copy(spc[:, 5:6], spc[:, 3:4]).then_inc("vn", 1)
                E.mark(("coeff", g), "vn")
                E.wait(("wy1", g))
                E.wait(("coeff", g))
                if "V" not in ST:
                    E.scalar_tensor_tensor(work[:, 0:C2],
                                           pWQ[:, 31::-1],
                                           1.0,
                                           vTn[:, t0 + C2 - 1: t0 - 1 if t0 > 0 else None: -1],
                                           ALU.mult, ALU.add)
                E.wait(("qkt", g))
                if "V" not in ST:
                    E.tensor_copy(c2t[0:32, 0:2], pWQ[0:32, 64:66])
                    E.tensor_copy(c2t[32:64, 0:2], pWQ[32:64, 66:68])
                E.tensor_copy(spc[0:32, 5:6], spc[0:32, 3:4]).then_inc("vn", 1)
                E.mark(("c2t", g), "vn")
                if "V" not in ST:
                    E.tensor_copy(spc[32:64, 5:6], c2t[32:64, 1:2])
                    E.scalar_tensor_tensor(dstA[0:32, :], pWQ[0:32, 160:192],
                                           c2t[0:32, 0:1], c_mn[0:32, :], ALU.subtract, ALU.add)
                    E.scalar_tensor_tensor(dstA[32:64, :], pWQ[32:64, 160:192],
                                           c2t[32:64, 1:2], c_mn[32:64, :], ALU.subtract, ALU.add)
                E.tensor_copy(spc[0:32, 5:6], spc[0:32, 3:4]).then_inc("vn", 1)
                E.mark(("dsta", g), "vn")
                if "V" not in ST:
                    for t in range(C2):
                        E.scalar_tensor_tensor(
                            trash[:, 0:2 * t + 1],
                            coeff[:, t * RPITCH: t * RPITCH + 2 * t + 1],
                            1.0,
                            work[:, C2 - 1 - t: C2 + t],
                            ALU.mult, ALU.mult,
                            accum_out=work[:, C2 + t: C2 + t + 1])
                E.tensor_copy(spc[:, 6:7], spc[:, 3:4])
                E.tensor_copy(spc[:, 7:8], spc[:, 3:4])
                E.tensor_copy(spc[:, 5:6], spc[:, 3:4]).then_inc("vn", 1)
                E.mark(("solve", g), "vn")
                E.wait(("dste", g))
                if "V" not in ST:
                    E.scalar_tensor_tensor(dqk[:, 0:32], dstA[0:32, :], -1.0,
                                           pWQ[0:32, 32:64], ALU.mult, ALU.mult)
                    E.scalar_tensor_tensor(dqk[:, 32:64], dstA[32:64, :], -1.0,
                                           pWQ[32:64, 32:64], ALU.mult, ALU.mult)
                E.tensor_copy(spc[0:32, 5:6], spc[0:32, 3:4]).then_inc("vn", 1)
                E.mark(("dqk", g), "vn")
                E.wait(("y2s", g))
                if "V" not in ST:
                    E.scalar_tensor_tensor(Sst[0:64, 0:64], Sst[0:64, 0:64], gmr[0:64, C2 - 1:C2],
                                           pS[0:64, 0:64], ALU.mult, ALU.add)
                    E.scalar_tensor_tensor(Sst[64:128, 64:128], Sst[64:128, 64:128],
                                           gmr[64:128, C2 - 1:C2],
                                           pS[64:128, 64:128], ALU.mult, ALU.add)
                E.tensor_copy(spc[:, 5:6], spc[:, 3:4]).then_inc("vn", 1)
                E.mark(("sstate", g), "vn")
                if "V" not in ST:
                    E.tensor_copy(ysS[:], pY[:])
                    E.tensor_copy(spc[:, 6:7], spc[:, 3:4])
                    E.tensor_tensor(ysqv[:], ysS[:], ysS[:], ALU.mult)
                E.tensor_copy(spc[:, 6:7], spc[:, 3:4])
                E.tensor_copy(spc[:, 5:6], spc[:, 3:4]).then_inc("vn", 1)
                E.mark(("ysq", g), "vn")
                E.wait(("rms1", g))
                E.tensor_copy(spc[:, 6:7], spc[:, 3:4])
                E.tensor_copy(spc[:, 7:8], spc[:, 3:4])
                if DBG == "r1":
                    if g > 0:
                        E.wait(("dbgd", g - 1))
                    E.tensor_copy(dbgt[0:2, :], pRS1[:]).then_inc("vn", 1)
                    E.mark(("dbgv", g), "vn")
                if "V" not in ST:
                    if _os.environ.get("DRS"):
                        E.tensor_scalar(rsq[:], pRSREP[:], 0.0, 1.0,
                                        ALU.mult, ALU.add)
                    else:
                        E.tensor_scalar(rsq[:], pRSREP[:], 1.0 / 64, 1e-6,
                                        ALU.mult, ALU.add)
                E.tensor_copy(spc[:, 6:7], spc[:, 3:4])
                E.tensor_copy(spr[0:2, 6:7], spr[0:2, 0:1]).then_inc("vn", 1)
                E.mark(("rms1v", g), "vn")
                if j + 1 < NCH2:
                    E.wait(("rep", g + 1))
                    if "V" not in ST:
                        E.scalar_tensor_tensor(cBr[:], pREP[:, 5 * 32:6 * 32], 1.0,
                                               spl[:, t0 + C2:t0 + 2 * C2],
                                               ALU.mult, ALU.subtract)
                        E.tensor_copy(c2r[:], pREP[:, 6 * 32:7 * 32])
                    E.tensor_copy(spc[:, 6:7], spc[:, 3:4]).then_inc("vn", 1)
                    E.mark(("cbr", g + 1), "vn")
                E.wait(("eg", g))
                if "V" not in ST:
                    E.tensor_scalar(den[:], eg[:, (j % 2) * C2:(j % 2) * C2 + C2],
                                    1.0, None, ALU.add)
                    E.tensor_copy(spc[:, 6:7], den[:, 0:1])
                    E.reciprocal(rec[:], den[:])
                    E.tensor_copy(spc[:, 6:7], rec[:, 0:1])
                    E.tensor_tensor(g2v[:], gT[:, t0:t0 + C2], rec[:], ALU.mult)
                    E.tensor_copy(spc[:, 6:7], spc[:, 3:4])
                    E.scalar_tensor_tensor(g2v[:], g2v[:], nwc[:, 0:1], ysS[:],
                                           ALU.mult, ALU.mult)
                E.tensor_copy(spc[:, 5:6], spc[:, 3:4]).then_inc("vn", 1)
                E.mark(("g2", g), "vn")
                E.wait(("rsr", g))
                if DBG == "rs":
                    if g > 0:
                        E.wait(("dbgd", g - 1))
                    E.tensor_copy(dbgt[:], rsq[:]).then_inc("vn", 1)
                    E.mark(("dbgv", g), "vn")
                if DBG == "r2":
                    if g > 0:
                        E.wait(("dbgd", g - 1))
                    E.tensor_copy(dbgt[0:2, :], rsr[:]).then_inc("vn", 1)
                    E.mark(("dbgv", g), "vn")
                if DBG == "g2" and g > 0:
                    E.wait(("dbgd", g - 1))
                if sp > 0 and j == 0:
                    E.wait(("ctxdma", sp - 1))
                if "V" not in ST:
                    E.tensor_tensor(ctxS[:, t0: t0 + C2], g2v[:], rsq[:], ALU.mult)
                E.tensor_copy(spc[:, 5:6], spc[:, 3:4]).then_inc("vn", 1)
                E.mark(("ctx", g), "vn")
        E.tensor_copy(spc[:, 5:6], ctxS[:, 0:1]).then_inc("vn", 1)
        E.mark(("alldone",), "vn")

    def run_pass(real):
        engs = {
            "sync": _Eng(None, "sync", marks, real, sems),
            "pe": _Eng(None, "pe", marks, real, sems),
            "v": _Eng(None, "v", marks, real, sems),
            "s": _Eng(None, "s", marks, real, sems),
            "g": _Eng(None, "g", marks, real, sems),
        }
        if not real:
            for k in marks.cnt:
                marks.cnt[k] = 0
            em_sync(engs["sync"]); em_pe(engs["pe"]); em_v(engs["v"])
            em_s(engs["s"]); em_gp(engs["g"])

    run_pass(False)
    _tab1 = dict(marks.tab)

    @block.sync
    def _(sync):
        e = _Eng(sync, "sync", marks, True, sems); marks.cnt.clear(); em_sync(e)

    @block.tensor
    def _(tensor):
        e = _Eng(tensor, "pe", marks, True, sems); marks.cnt["tn"] = 0; em_pe(e)

    @block.vector
    def _(vector):
        e = _Eng(vector, "v", marks, True, sems); marks.cnt["vn"] = 0; em_v(e)

    @block.scalar
    def _(scalar):
        e = _Eng(scalar, "s", marks, True, sems); marks.cnt["sn"] = 0; em_s(e)

    @block.gpsimd
    def _(gpsimd):
        e = _Eng(gpsimd, "g", marks, True, sems); marks.cnt["gn"] = 0; em_gp(e)

    es.close()
    _bad = [k for k in _tab1 if marks.tab.get(k) != _tab1[k]]
    assert not _bad, f"mark mismatch pass1 vs real: {_bad[:10]}"
    return nc


def host_L2_inputs(core, Wq, Wk, Wv, Wb, Wg, Wa, dt_bias, A_log, norm_w):
    r0 = 128 * core
    wcat = np.zeros((D, 768), np.float32)
    for m, W in enumerate([Wq, Wk, Wv, Wb, Wg]):
        wcat[:, m * 128:(m + 1) * 128] = W[r0:r0 + 128, :].T
    wcat[:, 640:642] = Wa[2 * core:2 * core + 2, :].T
    prm2 = np.zeros((128, 2), np.float32)
    for b in (0, 32, 64, 96):
        prm2[b:b + 2, 0] = dt_bias[2 * core:2 * core + 2]
        prm2[b:b + 2, 1] = A_log[2 * core:2 * core + 2]
    nwc = np.concatenate([norm_w, norm_w]).astype(np.float32).reshape(128, 1)
    c_ob = np.zeros((128, 128), np.float32)
    c_obn = np.zeros((128, 128), np.float32)
    for h in range(2):
        c_ob[h * 64:(h + 1) * 64, h * 64:(h + 1) * 64] = 1.0
        c_obn[h * 64:(h + 1) * 64, h * 64:(h + 1) * 64] = -1.0
    c_os = np.zeros((128, 128), np.float32)
    c_osn = np.zeros((128, 128), np.float32)
    c_os2 = np.zeros((128, 64), np.float32)
    for b in (0, 32, 64, 96):
        for h in range(2):
            c_os[b + h, h * 64:(h + 1) * 64] = 1.0
            c_osn[b + h, h * 64:(h + 1) * 64] = -1.0
            c_os2[b + h, h * 32:(h + 1) * 32] = 1.0
    c_o2 = np.zeros((128, 2), np.float32)
    c_o2[0:64, 0] = 1.0
    c_o2[64:128, 1] = 1.0
    c_mn = np.zeros((64, 32), np.float32)
    for h in range(2):
        for s in range(32):
            for t in range(32):
                if s > t:
                    c_mn[h * 32 + s, t] = -1e9
    idf32 = np.eye(128, dtype=np.float32)
    return {"wcat": wcat, "prm2": prm2, "nwc": nwc, "c_ob": c_ob, "c_obn": c_obn,
            "c_os": c_os, "c_osn": c_osn, "c_os2": c_os2, "c_o2": c_o2,
            "c_mn": c_mn, "idf32": idf32}
